# version 05
# baseline (speedup 1.0000x reference)
import math
import numpy as np
import ml_dtypes
import concourse.bass as bass
import concourse.mybir as mybir
from concourse.bass_utils import run_bass_kernel_spmd

F32 = mybir.dt.float32
BF16 = mybir.dt.bfloat16
U8 = mybir.dt.uint8
AF = mybir.ActivationFunctionType
ALU = mybir.AluOpType
AX = mybir.AxisListType

T = 2048
E = 1024
CE = 8
NH, DH, DR, DKV = 16, 64, 32, 256
HF = 4096
HC = 32
TTW = 512
NTT = T // TTW
NG, HG = 4, 4
QTN = T // 128
ISCALE = 1.0 / math.sqrt(DH + DR)
EPS = 1e-5
NEG = -1e30

ENGS = ("tensor", "vector", "scalar", "gpsimd", "sync")
SEMS = ("tD", "vD", "sD", "gD", "qS", "qG", "qC")

STAGE = 3


class Plan:
    def __init__(self):
        self.ops = {e: [] for e in ENGS}
        self.sem = {s: 0 for s in SEMS}

    def at(self, name):
        return self.sem[name]

    def do(self, eng, fn, waits=(), inc=(), tag=""):
        waits = tuple((s, v) for (s, v) in waits if v > 0)
        for s, a in inc:
            self.sem[s] += a
        self.ops[eng].append((waits, tuple(inc), fn, tag))

    def barrier(self):
        snap = [(s, self.sem[s]) for s in SEMS if self.sem[s] > 0]
        for e in ENGS:
            self.do(e, None, waits=snap, tag="barrier")

    def check_deadlock(self):
        idx = {e: 0 for e in ENGS}
        cnt = {s: 0 for s in self.sem}
        progress = True
        while progress:
            progress = False
            for e in ENGS:
                while idx[e] < len(self.ops[e]):
                    waits, inc, _, _ = self.ops[e][idx[e]]
                    if any(cnt[s] < v for (s, v) in waits):
                        break
                    for s, a in inc:
                        cnt[s] += a
                    idx[e] += 1
                    progress = True
        stuck = {e: idx[e] for e in ENGS if idx[e] < len(self.ops[e])}
        if stuck:
            lines = []
            for e in stuck:
                waits, _, _, tag = self.ops[e][idx[e]]
                lines.append(
                    f"{e}@{idx[e]}/{len(self.ops[e])} tag={tag} "
                    f"waits={[(s, v, cnt[s]) for s, v in waits]}")
            raise RuntimeError("PLAN DEADLOCK:\n" + "\n".join(lines))


class Arena:
    def __init__(self, ap, nbytes):
        self.ap = ap
        self.nbytes = nbytes
        self.free = [(0, nbytes)]
        self.regions = {}
        self.peak = 0

    def alloc(self, name, per_part_bytes, dtype, shape):
        assert name not in self.regions, name
        nb = (per_part_bytes + 31) & ~31
        for i, (off, sz) in enumerate(self.free):
            if sz >= nb:
                if sz == nb:
                    self.free.pop(i)
                else:
                    self.free[i] = (off + nb, sz - nb)
                view = self.ap[0:shape[0],
                               off:off + per_part_bytes].bitcast(dtype)
                want = per_part_bytes // mybir.dt.size(dtype)
                assert view.shape[1] == want, (name, view.shape, want)
                if len(shape) == 2:
                    assert shape[1] == want, (name, shape, want)
                self.regions[name] = (off, nb)
                used = self.nbytes - sum(s for _, s in self.free)
                self.peak = max(self.peak, used)
                return view
        raise RuntimeError(f"arena OOM {name} ({nb}B); free={self.free}")

    def freeit(self, *names):
        for name in names:
            off, nb = self.regions.pop(name)
            self.free.append((off, nb))
        self.free.sort()
        merged = []
        for off, sz in self.free:
            if merged and merged[-1][0] + merged[-1][1] == off:
                merged[-1] = (merged[-1][0], merged[-1][1] + sz)
            else:
                merged.append((off, sz))
        self.free = merged


def build_program(stage=STAGE, debug=False, sub=5):
    nc = bass.Bass("TRN2", target_bir_lowering=False,
                   detect_race_conditions=False)

    xT = nc.dram_tensor("xT", [E, T], F32, kind="ExternalInput").ap()
    wq = nc.dram_tensor("wq", [E, NH * DH], F32, kind="ExternalInput").ap()
    wqr = nc.dram_tensor("wqr", [E, NH * DR], F32, kind="ExternalInput").ap()
    wkrd = nc.dram_tensor("wkr", [E, DR], F32, kind="ExternalInput").ap()
    wkvdd = nc.dram_tensor("wkvd", [E, DKV], F32, kind="ExternalInput").ap()
    wkud = nc.dram_tensor("wku", [DKV, NH * DH], F32, kind="ExternalInput").ap()
    wvud = nc.dram_tensor("wvu", [DKV, NH * DH], F32, kind="ExternalInput").ap()
    wod = nc.dram_tensor("wo", [E, E], F32, kind="ExternalInput").ap()
    w1td = nc.dram_tensor("w1t", [E, HF], F32, kind="ExternalInput").ap()
    w2td = nc.dram_tensor("w2t", [HF, E], F32, kind="ExternalInput").ap()
    b1d = nc.dram_tensor("b1l", [128, HC], F32, kind="ExternalInput").ap()
    b2d = nc.dram_tensor("b2row", [1, E], F32, kind="ExternalInput").ap()
    scd = nc.dram_tensor("scalebc", [128, 1], F32, kind="ExternalInput").ap()
    lbd = nc.dram_tensor("lobobc", [128, NH], F32, kind="ExternalInput").ap()
    idd = nc.dram_tensor("ident", [128, 128], BF16, kind="ExternalInput").ap()
    m0d = nc.dram_tensor("mask0", [128, 128], F32, kind="ExternalInput").ap()
    mtd = nc.dram_tensor("maskt01", [128, 128], BF16, kind="ExternalInput").ap()
    ctd = nc.dram_tensor("ctab", [128, T], BF16, kind="ExternalInput").ap()
    s2d = nc.dram_tensor("stab2", [128, T], BF16, kind="ExternalInput").ap()
    outT = nc.dram_tensor("outT", [E, T], F32, kind="ExternalOutput").ap()
    xnd = nc.dram_tensor("xnd", [128, CE * T], BF16, kind="Internal").ap()
    dbg = {}
    if debug:
        dbg["xn"] = nc.dram_tensor("dbg_xn", [128, CE * TTW], BF16,
                                   kind="ExternalOutput").ap()
        dbg["g"] = nc.dram_tensor("dbg_g", [128, HC * TTW], BF16,
                                  kind="ExternalOutput").ap()
        dbg["al"] = nc.dram_tensor("dbg_al", [4, TTW], BF16,
                                   kind="ExternalOutput").ap()
        dbg["bt"] = nc.dram_tensor("dbg_bt", [16, TTW], BF16,
                                   kind="ExternalOutput").ap()
        dbg["lt"] = nc.dram_tensor("dbg_lt", [16, E], BF16,
                                   kind="ExternalOutput").ap()
        dbg["ub"] = nc.dram_tensor("dbg_ub", [128, TTW], F32,
                                   kind="ExternalOutput").ap()
        dbg["rb"] = nc.dram_tensor("dbg_rb", [128, TTW], F32,
                                   kind="ExternalOutput").ap()
        dbg["xn2"] = nc.dram_tensor("dbg_xn2", [128, CE * TTW], BF16,
                                    kind="ExternalOutput").ap()
        dbg["g2"] = nc.dram_tensor("dbg_g2", [128, HC * TTW], BF16,
                                   kind="ExternalOutput").ap()
        dbg["al2"] = nc.dram_tensor("dbg_al2", [4, TTW], BF16,
                                    kind="ExternalOutput").ap()
        dbg["y"] = nc.dram_tensor("dbg_y", [128, QTN * E], BF16,
                                  kind="ExternalOutput").ap()
        dbg["qh0"] = nc.dram_tensor("dbg_qh0", [96, T], BF16,
                                    kind="ExternalOutput").ap()
        dbg["kh0"] = nc.dram_tensor("dbg_kh0", [96, T], BF16,
                                    kind="ExternalOutput").ap()
        dbg["vg"] = nc.dram_tensor("dbg_vg", [128, 16 * HG * 65], BF16,
                                   kind="ExternalOutput").ap()
        dbg["st4"] = nc.dram_tensor("dbg_st4", [128, 40], F32,
                                    kind="ExternalOutput").ap()
        dbg["lat"] = nc.dram_tensor("dbg_lat", [128, 2 * T], BF16,
                                    kind="ExternalOutput").ap()
        dbg["kro"] = nc.dram_tensor("dbg_kro", [96, T], BF16,
                                    kind="ExternalOutput").ap()
        dbg["yscr"] = nc.dram_tensor("dbg_yscr", [128, 520], F32,
                                     kind="ExternalOutput").ap()
    w1bfd = nc.dram_tensor("w1bf", [128, CE * HF], BF16, kind="Internal").ap()
    w2bfd = nc.dram_tensor("w2bf", [HC, 128, E], BF16, kind="Internal").ap()

    ARENA_B = 212832
    arena_h = nc.alloc_sbuf_tensor("arena", [128, ARENA_B], U8)
    A = Arena(arena_h.ap(), ARENA_B)

    pb_ctx = [nc.psum_tensor(f"pb{i}", [128, 512], F32) for i in range(8)]
    pb = [c.__enter__()[:] for c in pb_ctx]

    P = Plan()

    stream = A.alloc("stream", T * 4 * CE, F32, [128, CE * T])
    ident = A.alloc("ident", 128 * 2, BF16, [128, 128])
    b1l = A.alloc("b1l", HC * 4, F32, [128, HC])
    b2row = A.alloc("b2row", E * 4, F32, [1, E])
    b2hl = A.alloc("b2hl", E * 2, BF16, [2, E])
    b2tmp = A.alloc("b2tmp", E * 2, BF16, [1, E])
    onesr = A.alloc("onesr", TTW * 2, BF16, [2, TTW])
    ones128 = A.alloc("ones128", 128 * 2, BF16, [128, 128])
    scalebc = A.alloc("scalebc", 32, F32, [128, 8])
    epsc = A.alloc("epsc", 32, F32, [128, 8])

    def fdma(outap, inap, slow=False):
        def f(eng):
            if slow:
                return eng.dma_start(out=outap, in_=inap,
                                     allow_slow_non_contiguous=True)
            return eng.dma_start(out=outap, in_=inap)
        return f

    st_xq = []
    for xt in range(NTT):
        P.do("sync",
             fdma(stream.rearrange("p (c n) -> p c n", c=CE)
                  [:, :, xt * TTW:(xt + 1) * TTW],
                  xT.rearrange("(c p) n -> p c n", p=128)
                  [:, :, xt * TTW:(xt + 1) * TTW]),
             inc=[("qS", 16)], tag=f"ld_x{xt}")
        st_xq.append(P.at("qS"))
    for ap_, src in ((ident, idd), (b1l, b1d), (b2row, b2d),
                     (scalebc[:, 0:1], scd)):
        P.do("sync", fdma(ap_, src), inc=[("qS", 16)], tag="ld_const")
    q_consts = P.at("qS")

    P.do("vector", lambda eng: eng.memset(onesr, 1.0), inc=[("vD", 1)])
    P.do("vector", lambda eng: eng.memset(epsc, EPS), inc=[("vD", 1)])
    P.do("vector", lambda eng: eng.memset(ones128, 1.0), inc=[("vD", 1)])

    def f_b2(eng):
        eng.tensor_copy(b2hl[0:1, :], b2row)
        eng.nop(cycle_cnt=400)
        return eng.tensor_tensor(b2tmp, b2row, b2hl[0:1, :], ALU.subtract)
    P.do("vector", f_b2, waits=[("qS", q_consts)], inc=[("vD", 1)], tag="b2")
    P.do("sync", fdma(b2hl[1:2, :], b2tmp), waits=[("vD", P.at("vD"))],
         inc=[("qS", 16)], tag="b2lo_mv")
    A.freeit("b2tmp")
    v_init = P.at("vD")
    s_init = P.at("sD")
    st_q_init = P.at("qS")
    A.freeit("b2row")

    st = dict(prev_resid=0, ln_prev_t=0, w2cache_q=0)

    def emit_ln_pre(tt, scr, cast_waits):
        xb2, x22, ub, rb, t1s, t1g = scr
        out = []
        for c in range(CE):
            pcol = slice(c * TTW, (c + 1) * TTW)
            src = stream[:, c * T + tt * TTW: c * T + (tt + 1) * TTW]

            def f(eng, src=src, dst=xb2[:, pcol]):
                return eng.tensor_copy(dst, src)
            P.do("vector", f, waits=cast_waits(c), inc=[("vD", 1)],
                 tag=f"lncast{c}")
            v_c = P.at("vD")

            def f(eng, i=xb2[:, pcol], o=x22[:, pcol]):
                return eng.activation(o, i, AF.Square)
            P.do("scalar", f, waits=[("vD", v_c)] + cast_waits(c + 10),
                 inc=[("sD", 1)], tag=f"lnsq{c}")
            out.append(P.at("sD"))
        return out

    def emit_ln_post(tt, xn, scr, s_sq, mm_waits):
        xb2, x22, ub, rb, t1s, t1g = scr
        for c in range(CE):
            pcol = slice(c * TTW, (c + 1) * TTW)
            w = [("sD", s_sq[c])]
            if c == 0:
                w += mm_waits

            def f(eng, c=c, pcol=pcol):
                eng.matmul(pb[0], ones128, xb2[:, pcol],
                           start=(c == 0), stop=(c == CE - 1))
                return eng.matmul(pb[1], ones128, x22[:, pcol],
                                  start=(c == 0), stop=(c == CE - 1))
            P.do("tensor", f, waits=w, inc=[("tD", 1)], tag=f"lnmm{c}")
        st["ln_prev_t"] = P.at("tD")
        t_stats = P.at("tD")

        def f(eng):
            return eng.tensor_scalar(ub, pb[0][:, 0:TTW], 1.0 / E, None,
                                     ALU.mult)
        P.do("vector", f, waits=[("tD", t_stats)], inc=[("vD", 1)],
             tag="lnst1")
        v1 = P.at("vD")
        P.do("scalar", lambda eng: eng.activation(t1s, ub, AF.Square),
             waits=[("vD", v1)], inc=[("sD", 1)], tag="lnst2")
        s1 = P.at("sD")
        P.do("vector",
             lambda eng: eng.scalar_tensor_tensor(
                 t1s, pb[1][:, 0:TTW], 1.0 / E, t1s, ALU.mult, ALU.subtract),
             waits=[("sD", s1)], inc=[("vD", 1)], tag="lnst3")
        v2 = P.at("vD")

        def f(eng):
            eng.activation(t1g, t1s, AF.Ln, bias=epsc[:, 0:1])
            return eng.activation(rb, t1g, AF.Exp, scale=-0.5)
        P.do("scalar", f, waits=[("vD", v2)], inc=[("sD", 1)], tag="lnst4")
        s_rb = P.at("sD")

        vnorm = []
        for c in range(CE):
            src = stream[:, c * T + tt * TTW: c * T + (tt + 1) * TTW]
            dst = xn[:, c * TTW:(c + 1) * TTW]

            def f(eng, src=src, dst=dst):
                eng.tensor_tensor(t1s, src, ub, ALU.subtract)
                return eng.tensor_tensor(dst, t1s, rb, ALU.mult)
            P.do("vector", f, waits=[("sD", s_rb)], inc=[("vD", 1)],
                 tag=f"lnnv{c}")
            vnorm.append(P.at("vD"))
        return [("vD", P.at("vD"))], vnorm

    def emit_ffn(call):
        g_sb = A.alloc("g", HC * TTW * 2, BF16, [128, HC * TTW])
        xb2 = g_sb[:, 0:CE * TTW]
        x22 = g_sb[:, CE * TTW:2 * CE * TTW]
        xn = A.alloc("xn", CE * TTW * 2, BF16, [128, CE * TTW])
        ub = A.alloc("ub", TTW * 4, F32, [128, TTW])
        rb = A.alloc("rb", TTW * 4, F32, [128, TTW])
        t1s = A.alloc("t1s", TTW * 4, F32, [128, TTW])
        t1g = A.alloc("t1g", TTW * 4, F32, [128, TTW])
        w2c = A.alloc("w2c", 3 * E * 2, BF16, [128, 3 * E])
        lt = g_sb[0:16, 16 * TTW:16 * TTW + E]
        xn16 = A.alloc("xn16", CE * 16 * 2, BF16, [128, CE * 16])
        alphas = A.alloc("alphas", TTW * 2, BF16, [4, TTW])
        bt = [A.alloc(f"bt{j}", TTW * 2, BF16, [4, TTW]) for j in range(4)]
        e4a = A.alloc("e4a", HF * 2, BF16, [4, HF])
        e16j = [A.alloc(f"e16j{j}", E * 2, BF16, [4, E]) for j in range(4)]
        scr = (xb2, x22, ub, rb, t1s, t1g)
        if call == 0:
            w1p = [A.alloc(f"w1t{h}", 4 * HF * 2, BF16, [128, 4 * HF])
                   for h in range(2)]
        else:
            w1p = st["w1t2"]

        if call == 0:
            for h in range(2):
                P.do("gpsimd",
                     fdma(w1p[h].rearrange("p (c n) -> p c n", c=4),
                          w1td[h * 512:(h + 1) * 512, :]
                          .rearrange("(c p) n -> p c n", p=128)),
                     waits=[("vD", v_init)], inc=[("qG", 16)],
                     tag=f"ld_w1_{h}")
            w1_ready = ("qG", P.at("qG"))
            for h in range(2):
                P.do("scalar",
                     fdma(w1bfd[:, h * 4 * HF:(h + 1) * 4 * HF], w1p[h]),
                     waits=[w1_ready], inc=[("qC", 16)], tag=f"w1cache{h}")
            st["w1cache_q"] = ("qC", P.at("qC"))
        else:
            w1_ready = st["w1t2_ready"]

        s_aev = 0
        t_w2used = {}
        pctr = 0
        v_xn16 = 0
        s_sq_next = None
        for tt in range(NTT):
            if s_sq_next is None:
                def cw(c, tt=tt):
                    w = [("tD", st["ln_prev_t"]),
                         ("vD", st.get("early_resid", 0))]
                    if call == 0:
                        w.append(("qS", st_xq[tt]))
                    return w
                s_sq_next = emit_ln_pre(tt, scr, cw)
            mmw = [("vD", st.get("early_resid", 0)), ("vD", v_init),
                   ("qS", q_consts)]
            xn_ready, vnorm = emit_ln_post(tt, xn, scr, s_sq_next, mmw)
            s_sq_next = None

            if tt == 0:
                def f(eng):
                    return eng.tensor_copy(
                        xn16.rearrange("p (c n) -> p c n", c=CE),
                        xn.rearrange("p (c n) -> p c n", c=CE)[:, :, 1:17])
                P.do("vector", f, waits=xn_ready, inc=[("vD", 1)], tag="xn16")
                v_xn16 = P.at("vD")
                s_prev = s_aev
                for c in range(CE):
                    def f(eng, c=c):
                        return eng.matmul(pb[2][0:16, 0:128],
                                          xn[:, c * TTW + 1:c * TTW + 17],
                                          ident, start=True, stop=True)
                    P.do("tensor", f,
                         waits=xn_ready + [("sD", s_prev)],
                         inc=[("tD", 1)], tag=f"ltmm{c}")
                    t_lt = P.at("tD")

                    def f(eng, c=c):
                        return eng.copy(lt[0:16, c * 128:(c + 1) * 128],
                                        pb[2][0:16, 0:128])
                    P.do("scalar", f, waits=[("tD", t_lt)], inc=[("sD", 1)],
                         tag=f"ltev{c}")
                    s_prev = P.at("sD")
                s_aev = s_prev
                s_lt = s_prev
                P.do("vector", lambda eng: eng.memset(e4a, 0.0),
                     inc=[("vD", 1)], tag="e4a0")
                for j in range(4):
                    P.do("vector",
                         lambda eng, j=j: eng.memset(e16j[j], 0.0),
                         inc=[("vD", 1)], tag=f"e16z{j}")
                v_ez = P.at("vD")
                for i in range(4):
                    P.do("sync",
                         fdma(e4a[i:i + 1, 1024 * i:1024 * (i + 1)],
                              lt[2 * i:2 * i + 1, :]),
                         waits=[("sD", s_lt), ("vD", v_ez)],
                         inc=[("qS", 16)], tag=f"e4a{i}")
                for i4 in range(4):
                    for j in range(4):
                        P.do("sync",
                             fdma(e16j[j][i4:i4 + 1,
                                          256 * i4:256 * i4 + 256],
                                  lt[8 + 2 * i4:9 + 2 * i4, j:j + 1021:4],
                                  slow=True),
                             waits=[("sD", s_lt), ("vD", v_ez)],
                             inc=[("qS", 16)], tag=f"e16_{i4}_{j}")
                q_e = P.at("qS")

            for c in range(CE):
                w = [("vD", vnorm[c]), ("vD", v_xn16)]
                if c == 0:
                    w.append(("sD", s_aev))

                def f(eng, c=c):
                    return eng.matmul(
                        pb[2][0:4, 0:TTW],
                        xn16[:, c * 16 + 1:c * 16 + 8:2],
                        xn[:, c * TTW:(c + 1) * TTW],
                        start=(c == 0), stop=(c == CE - 1))
                P.do("tensor", f, waits=w, inc=[("tD", 1)], tag=f"amm{c}")
            t_a = P.at("tD")

            def f(eng):
                return eng.activation(alphas, pb[2][0:4, 0:TTW], AF.Copy,
                                      scale=scalebc[0:4, 0:1])
            P.do("scalar", f, waits=[("tD", t_a)], inc=[("sD", 1)], tag="aev")
            s_aev = P.at("sD")

            s_gelu = {}
            for m in range(HC):
                par = m % 2
                if m < 2:
                    for kc in range(CE):
                        w = [("vD", vnorm[kc])]
                        if m == 0 and kc == 0:
                            w += [("sD", s_aev), w1_ready, ("qS", q_e)]

                        def f(eng, m=m, par=par, kc=kc):
                            wsl = w1p[kc // 4][:, (kc % 4) * HF + m * 128:
                                               (kc % 4) * HF + (m + 1) * 128]
                            return eng.matmul(
                                pb[par], wsl,
                                xn[:, kc * TTW:(kc + 1) * TTW],
                                start=(kc == 0), stop=False)
                        P.do("tensor", f, waits=w, inc=[("tD", 1)],
                             tag=f"hmm{m}_{kc}")

                    def f(eng, m=m, par=par):
                        return eng.matmul(
                            pb[par], e4a[0:4, m * 128:(m + 1) * 128],
                            alphas[0:4, :], start=False, stop=True)
                    P.do("tensor", f, waits=[("sD", s_aev)],
                         inc=[("tD", 1)], tag=f"hmm{m}_l")
                else:
                    w = list(xn_ready) + [("sD", s_aev), w1_ready]
                    w.append(("sD", s_gelu[m - 2]))

                    def f(eng, m=m, par=par):
                        for kc in range(CE):
                            wsl = w1p[kc // 4][:, (kc % 4) * HF + m * 128:
                                               (kc % 4) * HF + (m + 1) * 128]
                            eng.matmul(
                                pb[par], wsl,
                                xn[:, kc * TTW:(kc + 1) * TTW],
                                start=(kc == 0), stop=False)
                        return eng.matmul(
                            pb[par], e4a[0:4, m * 128:(m + 1) * 128],
                            alphas[0:4, :], start=False, stop=True)
                    P.do("tensor", f, waits=w, inc=[("tD", 1)], tag=f"hmm{m}")
                t_h = P.at("tD")

                def f(eng, m=m, par=par):
                    return eng.activation(g_sb[:, m * TTW:(m + 1) * TTW],
                                          pb[par], AF.Gelu,
                                          bias=b1l[:, m:m + 1])
                P.do("scalar", f, waits=[("tD", t_h)], inc=[("sD", 1)],
                     tag=f"gelu{m}")
                s_gelu[m] = P.at("sD")
            s_gelu_all = P.at("sD")

            for j in range(4):
                for c in range(CE):
                    kc = 8 * j + c
                    w = [("sD", s_gelu[kc])]
                    if c == 0:
                        w.append(("sD", s_aev))

                    def f(eng, j=j, c=c, kc=kc):
                        return eng.matmul(
                            pb[2][0:4, 0:TTW],
                            xn16[:, c * 16 + 9:c * 16 + 16:2],
                            g_sb[:, kc * TTW:(kc + 1) * TTW],
                            start=(c == 0), stop=(c == CE - 1))
                    P.do("tensor", f, waits=w, inc=[("tD", 1)],
                         tag=f"bmm{j}_{c}")
                t_b = P.at("tD")

                def f(eng, j=j):
                    return eng.activation(bt[j][0:4, :],
                                          pb[2][0:4, 0:TTW], AF.Copy,
                                          scale=scalebc[0:4, 0:1])
                P.do("scalar", f, waits=[("tD", t_b)], inc=[("sD", 1)],
                     tag=f"bev{j}")
                s_aev = P.at("sD")
            s_beta = P.at("sD")

            first = (call == 0 and tt == 0)
            qdma = {}
            qcache = {}
            NBUF = 3

            def w2buf(buf):
                return w2c[:, buf * E:(buf + 1) * E]

            def plan_w2_dma(i):
                buf = (base_p + i) % NBUF
                w = []
                if base_p + i >= NBUF:
                    w.append(("tD", t_w2used[base_p + i - NBUF]))
                if buf >= 2:
                    w.append(("sD", s_gelu_all))
                if first and i >= NBUF:
                    w.append(("qC", qcache[i - NBUF]))
                dst = w2buf(buf)
                if first:
                    P.do("gpsimd",
                         fdma(dst, w2td[i * 128:(i + 1) * 128, :]),
                         waits=w, inc=[("qG", 16)], tag=f"w2l{i}")
                    qdma[i] = ("qG", P.at("qG"))
                    P.do("scalar", fdma(w2bfd[i], dst),
                         waits=[qdma[i]], inc=[("qC", 16)], tag=f"w2c{i}")
                    qcache[i] = P.at("qC")
                else:
                    P.do("sync", fdma(dst, w2bfd[i]),
                         waits=w + [("qC", st["w2cache_q"])],
                         inc=[("qS", 16)], tag=f"w2l{i}")
                    qdma[i] = ("qS", P.at("qS"))

            def plan_w2_mm(kc):
                buf = (base_p + kc) % NBUF
                w = [qdma[kc]]
                if kc == 0:
                    w += [("sD", s_gelu_all), ("sD", s_beta),
                          ("vD", st["prev_resid"])]
                else:
                    w.append(("sD", s_gelu[kc]))
                src = w2buf(buf)

                def f(eng, kc=kc, src=src):
                    r = None
                    for m in range(8):
                        r = eng.matmul(
                            pb[m],
                            src[:, m * 128:(m + 1) * 128],
                            g_sb[:, kc * TTW:(kc + 1) * TTW],
                            start=(kc == 0), stop=False)
                    return r
                P.do("tensor", f, waits=w, inc=[("tD", 1)], tag=f"omm{kc}")
                t_w2used[base_p + kc] = P.at("tD")

            base_p = pctr
            for i in range(HC + 2):
                if i < HC:
                    plan_w2_dma(i)
                if i >= 2:
                    plan_w2_mm(i - 2)
            pctr = base_p + HC
            if first:
                st["w2cache_q"] = max(qcache.values())

            if tt + 1 < NTT:
                def cw(c, bp=base_p, tt1=tt + 1):
                    if c < 10:
                        w = [("tD", t_w2used[bp + c])]
                    else:
                        w = [("tD", t_w2used[bp + 8 + (c - 10)])]
                    if call == 0:
                        w.append(("qS", st_xq[tt1]))
                    return w
                s_sq_next = emit_ln_pre(tt + 1, scr, cw)

            def f(eng):
                r = None
                for m in range(8):
                    for j in range(4):
                        r = eng.matmul(pb[m],
                                       e16j[j][0:4, m * 128:(m + 1) * 128],
                                       bt[j][0:4, :],
                                       start=False, stop=False)
                    r = eng.matmul(pb[m], b2hl[0:2, m * 128:(m + 1) * 128],
                                   onesr[0:2, :], start=False, stop=True)
                return r
            P.do("tensor", f, waits=[("sD", s_beta), ("vD", v_init),
                                     ("qS", max(q_e, st_q_init))],
                 inc=[("tD", 1)], tag="oxtr")
            t_out = P.at("tD")

            for m in range(8):
                dst = stream[:, m * T + tt * TTW:m * T + (tt + 1) * TTW]

                def f(eng, m=m, dst=dst):
                    return eng.scalar_tensor_tensor(
                        dst, pb[m], 0.5, dst, ALU.mult, ALU.add)
                P.do("vector", f, waits=[("tD", t_out)], inc=[("vD", 1)],
                     tag=f"res{m}")
                if m == 1:
                    st["early_resid"] = P.at("vD")
            st["prev_resid"] = P.at("vD")
            st["ln_t_gate"] = t_out

        A.freeit("g", "xn", "ub", "rb", "t1s", "t1g",
                 "w2c", "xn16", "alphas", "e4a",
                 *(("w1t0", "w1t1") if call == 0 else ("w1u0", "w1u1")),
                 *[f"bt{j}" for j in range(4)],
                 *[f"e16j{j}" for j in range(4)])

    def emit_attn(sub=5):
        P.barrier()
        mask0 = A.alloc("mask0", 128 * 4, F32, [128, 128])
        maskt01 = A.alloc("maskt01", 128 * 2, BF16, [128, 128])
        cbc = A.alloc("cbc", NH * 4, F32, [128, NH])
        P.do("sync", fdma(mask0, m0d), inc=[("qS", 16)], tag="ld_mask0")
        P.do("sync", fdma(maskt01, mtd), inc=[("qS", 16)], tag="ld_maskt")
        P.do("sync", fdma(cbc, lbd), inc=[("qS", 16)], tag="ld_lobo")
        q_m = P.at("qS")
        xn = A.alloc("xn", CE * TTW * 2, BF16, [128, CE * TTW])
        xb2 = A.alloc("xb2", CE * TTW * 2, BF16, [128, CE * TTW])
        x22 = A.alloc("x22", CE * TTW * 2, BF16, [128, CE * TTW])
        ub = A.alloc("ub", TTW * 4, F32, [128, TTW])
        rb = A.alloc("rb", TTW * 4, F32, [128, TTW])
        t1s = A.alloc("t1s", TTW * 4, F32, [128, TTW])
        t1g = A.alloc("t1g", TTW * 4, F32, [128, TTW])
        scr = (xb2, x22, ub, rb, t1s, t1g)
        latent = A.alloc("latent", 2 * T * 2, BF16, [128, 2 * T])
        krope = A.alloc("krope", T * 2, BF16, [96, T])
        ctab = A.alloc("ctab", T * 2, BF16, [128, T])
        stab2 = A.alloc("stab2", T * 2, BF16, [128, T])
        ro1 = A.alloc("ro1", TTW * 4, F32, [128, TTW])
        ro2 = A.alloc("ro2", TTW * 4, F32, [128, TTW])
        ro3 = A.alloc("ro3", TTW * 4, F32, [128, TTW])
        wkvd_sb = A.alloc("wkvd_sb", CE * DKV * 2, BF16, [128, CE * DKV])
        wkr_sb = A.alloc("wkr_sb", CE * DR * 2, BF16, [128, CE * DR])

        P.do("sync", fdma(ctab, ctd), inc=[("qS", 16)], tag="ld_ctab")
        P.do("sync", fdma(stab2, s2d), inc=[("qS", 16)], tag="ld_stab")
        q_tabs = P.at("qS")
        P.do("gpsimd", fdma(wkvd_sb.rearrange("p (c n) -> p c n", c=CE),
                            wkvdd.rearrange("(c p) n -> p c n", p=128)),
             inc=[("qG", 16)], tag="ld_wkvd")
        P.do("gpsimd", fdma(wkr_sb.rearrange("p (c n) -> p c n", c=CE),
                            wkrd.rearrange("(c p) n -> p c n", p=128)),
             inc=[("qG", 16)], tag="ld_wkr")
        q_kv = P.at("qG")

        s_latent_ev = {}
        v_krope = 0
        q_spill = [0] * NTT
        s_sq_next = None
        for tt in range(NTT):
            if s_sq_next is None:
                def cw(c):
                    return [("tD", st["ln_prev_t"]),
                            ("vD", st.get("early_resid", 0))]
                s_sq_next = emit_ln_pre(tt, scr, cw)
            mmw = [("vD", st.get("early_resid", 0)), ("vD", v_init),
                   ("qS", q_consts)]
            xn_ready, vnorm = emit_ln_post(tt, xn, scr, s_sq_next, mmw)
            s_sq_next = None
            P.do("sync",
                 fdma(xnd.rearrange("p (c n) -> p c n", c=CE)
                      [:, :, tt * TTW:(tt + 1) * TTW],
                      xn.rearrange("p (c n) -> p c n", c=CE)),
                 waits=xn_ready, inc=[("qS", 16)], tag=f"xnspill{tt}")
            q_spill[tt] = P.at("qS")
            if tt + 1 < NTT:
                def cw(c):
                    return [("tD", st["ln_prev_t"])]
                s_sq_next = emit_ln_pre(tt + 1, scr, cw)
            for mc in range(2):
                for kc in range(CE):
                    w = list(xn_ready) if kc == 0 else []
                    if kc == 0:
                        w += [("qG", q_kv),
                              ("sD", s_latent_ev.get(mc, 0))]

                    def f(eng, mc=mc, kc=kc):
                        return eng.matmul(
                            pb[2 + mc],
                            wkvd_sb[:, kc * DKV + mc * 128:
                                    kc * DKV + (mc + 1) * 128],
                            xn[:, kc * TTW:(kc + 1) * TTW],
                            start=(kc == 0), stop=(kc == CE - 1))
                    P.do("tensor", f, waits=w, inc=[("tD", 1)],
                         tag=f"lat{mc}_{kc}")
                t_l = P.at("tD")

                def f(eng, mc=mc, tt=tt):
                    return eng.copy(
                        latent[:, mc * T + tt * TTW:mc * T + (tt + 1) * TTW],
                        pb[2 + mc])
                P.do("scalar", f, waits=[("tD", t_l)], inc=[("sD", 1)],
                     tag=f"latev{mc}")
                s_latent_ev[mc] = P.at("sD")
            for kc in range(CE):
                w = list(xn_ready) if kc == 0 else []
                if kc == 0:
                    w += [("qG", q_kv), ("vD", v_krope)]

                def f(eng, kc=kc):
                    return eng.matmul(
                        pb[4][64:96, 0:TTW],
                        wkr_sb[:, kc * DR:(kc + 1) * DR],
                        xn[:, kc * TTW:(kc + 1) * TTW],
                        start=(kc == 0), stop=(kc == CE - 1))
                P.do("tensor", f, waits=w, inc=[("tD", 1)], tag=f"krmm{kc}")
            t_kr = P.at("tD")
            cs = slice(tt * TTW, (tt + 1) * TTW)

            def f(eng, cs=cs):
                eng.tensor_scalar(ro3[64:96, :], pb[4][64:96, 0:TTW], ISCALE,
                                  None, ALU.mult)
                eng.tensor_tensor(ro1[64:96, :], ro3[64:96, :],
                                  ctab[64:96, cs], ALU.mult)
                eng.tensor_tensor(ro2[64:96, :], ro3[64:96, :],
                                  stab2[64:96, cs], ALU.mult)
                mask = [(p + 16) % 32 for p in range(32)]
                eng.stream_shuffle(ro3[64:96, :], ro2[64:96, :], mask)
                return eng.tensor_tensor(krope[64:96, cs], ro1[64:96, :],
                                         ro3[64:96, :], ALU.add)
            P.do("vector", f, waits=[("tD", t_kr), ("qS", q_tabs)],
                 inc=[("vD", 1)], tag=f"krrope{tt}")
            v_krope = P.at("vD")
        v_krope_all = P.at("vD")
        A.freeit("xn", "xb2", "x22", "ub", "rb", "t1s", "t1g",
                 "wkvd_sb", "wkr_sb")

        if sub < 1.5:
            A.freeit("mask0", "maskt01", "cbc", "latent", "krope",
                     "ctab", "stab2", "ro1", "ro2", "ro3")
            return
        P.barrier()
        y_sb = A.alloc("y", QTN * E * 2, BF16, [128, QTN * E])
        qh = [A.alloc(f"qh{j}", T * 2, BF16, [96, T]) for j in range(HG)]
        kh = [A.alloc(f"kh{j}", T * 2, BF16, [96, T]) for j in range(HG)]
        v_g = A.alloc("v_g", 16 * HG * 65 * 2, BF16, [128, 16 * HG * 65])
        v_g4 = v_g.rearrange("p (c h n) -> p c h n", c=16, h=HG)
        xnb = [A.alloc(f"xnb{j}", CE * TTW * 2, BF16, [128, CE * TTW])
               for j in range(2)]
        wq_g = A.alloc("wq_g", CE * 256 * 2, BF16, [128, CE * 256])
        wqr_g = A.alloc("wqr_g", CE * 128 * 2, BF16, [128, CE * 128])
        wku_g = A.alloc("wku_g", 2 * 256 * 2, BF16, [128, 2 * 256])
        wvu_g = A.alloc("wvu_g", 2 * 256 * 2, BF16, [128, 2 * 256])
        ptb = [A.alloc(f"ptb{j}", 512 * 2, BF16, [128, 512])
               for j in range(2)]
        yscr = [A.alloc(f"yscr{j}", 260 * 4, F32, [128, 260])
                for j in range(2)]
        mpart = A.alloc("mpart", 16 * 4, F32, [128, 16])
        m4v = [A.alloc(f"m4v{j}", 16, F32, [128, 4]) for j in range(2)]
        emv = [A.alloc(f"emv{j}", 16, F32, [128, 4]) for j in range(2)]
        zcv = [A.alloc(f"zcv{j}", 16, F32, [128, 4]) for j in range(2)]
        dn1 = [A.alloc(f"dn1{j}", 16, F32, [128, 4]) for j in range(2)]
        dn2 = [A.alloc(f"dn2{j}", 16, F32, [128, 4]) for j in range(2)]
        rdv = [A.alloc(f"rdv{j}", 16, F32, [128, 4]) for j in range(2)]

        def f(eng):
            return eng.memset(v_g4[:, :, :, 64:65], 1.0)
        P.do("vector", f, inc=[("vD", 1)], tag="vones")
        v_vones = P.at("vD")

        t_flash_done = 0
        s_norm_done = 0
        t_projmm_done = 0
        t_projuse = {}
        s_pevac = {}
        v_pevac = {}
        ngrun = 1 if sub < 3.5 else NG
        for g in range(ngrun):
            w = [("tD", t_projmm_done)]
            P.do("gpsimd", fdma(wq_g.rearrange("p (c n) -> p c n", c=CE),
                                wq[:, 256 * g:256 * (g + 1)]
                                .rearrange("(c p) n -> p c n", p=128)),
                 waits=w, inc=[("qG", 16)], tag=f"gwq{g}")
            P.do("gpsimd", fdma(wqr_g.rearrange("p (c n) -> p c n", c=CE),
                                wqr[:, 128 * g:128 * (g + 1)]
                                .rearrange("(c p) n -> p c n", p=128)),
                 waits=w, inc=[("qG", 16)], tag=f"gwqr{g}")
            P.do("gpsimd", fdma(wku_g.rearrange("p (c n) -> p c n", c=2),
                                wkud[:, 256 * g:256 * (g + 1)]
                                .rearrange("(c p) n -> p c n", p=128)),
                 waits=w, inc=[("qG", 16)], tag=f"gwku{g}")
            P.do("gpsimd", fdma(wvu_g.rearrange("p (c n) -> p c n", c=2),
                                wvud[:, 256 * g:256 * (g + 1)]
                                .rearrange("(c p) n -> p c n", p=128)),
                 waits=w, inc=[("qG", 16)], tag=f"gwvu{g}")
            q_gw = P.at("qG")

            q_xnb = [0, 0]
            for tt in range(NTT):
                buf = tt % 2
                w = [("qS", q_spill[tt]), ("tD", t_projmm_done)]
                if tt >= 2:
                    w.append(("tD", t_projuse[tt - 2]))

                def f(eng, tt=tt, buf=buf):
                    return eng.dma_start(
                        out=xnb[buf].rearrange("p (c n) -> p c n", c=CE),
                        in_=xnd.rearrange("p (c n) -> p c n", c=CE)
                        [:, :, tt * TTW:(tt + 1) * TTW])
                P.do("sync", f, waits=w, inc=[("qS", 16)],
                     tag=f"xnl{g}_{tt}")
                q_xnb[buf] = P.at("qS")

                xg = xnb[buf]
                cs = slice(tt * TTW, (tt + 1) * TTW)
                basew = [("qS", q_xnb[buf]), ("qG", q_gw),
                         ("tD", t_flash_done), ("sD", s_norm_done)]
                if sub < 1.65:
                    t_projuse[tt] = P.at("tD")
                    continue
                if sub >= 1.65:
                    for hp in range(HG):
                        bank = hp % 2
                        for kc in range(CE):
                            w = list(basew) if (kc == 0) else []
                            if kc == 0:
                                w.append(("sD", s_pevac.get(("qc", bank), 0)))

                            def f(eng, hp=hp, kc=kc, xg=xg, bank=bank):
                                return eng.matmul(
                                    pb[bank][0:64, 0:TTW],
                                    wq_g[:, kc * 256 + 64 * hp:
                                         kc * 256 + 64 * hp + 64],
                                    xg[:, kc * TTW:(kc + 1) * TTW],
                                    start=(kc == 0), stop=(kc == CE - 1))
                            P.do("tensor", f, waits=w, inc=[("tD", 1)],
                                 tag=f"qc{g}_{tt}_{hp}_{kc}")
                        t_qc = P.at("tD")

                        def f(eng, hp=hp, bank=bank, cs=cs):
                            return eng.copy(qh[hp][0:64, cs],
                                            pb[bank][0:64, 0:TTW])
                        P.do("scalar", f, waits=[("tD", t_qc)],
                             inc=[("sD", 1)], tag=f"qcev{g}_{tt}_{hp}")
                        s_pevac[("qc", bank)] = P.at("sD")
                if sub < 1.75:
                    t_projuse[tt] = P.at("tD")
                    continue
                for hp in range(HG):
                    bank = 2 + hp % 2
                    for kc in range(CE):
                        w = list(basew) if kc == 0 else []
                        if kc == 0:
                            w.append(("vD", v_pevac.get(("qr", bank), 0)))

                        def f(eng, hp=hp, kc=kc, xg=xg, bank=bank):
                            return eng.matmul(
                                pb[bank][64:96, 0:TTW],
                                wqr_g[:, kc * 128 + 32 * hp:
                                      kc * 128 + 32 * hp + 32],
                                xg[:, kc * TTW:(kc + 1) * TTW],
                                start=(kc == 0), stop=(kc == CE - 1))
                        P.do("tensor", f, waits=w, inc=[("tD", 1)],
                             tag=f"qr{g}_{tt}_{hp}_{kc}")
                    t_qr = P.at("tD")

                    def f(eng, hp=hp, bank=bank, cs=cs):
                        eng.tensor_tensor(ro1[64:96, :], pb[bank][64:96, 0:TTW],
                                          ctab[64:96, cs], ALU.mult)
                        eng.tensor_tensor(ro2[64:96, :], pb[bank][64:96, 0:TTW],
                                          stab2[64:96, cs], ALU.mult)
                        mask = [(p + 16) % 32 for p in range(32)]
                        eng.stream_shuffle(ro3[64:96, :], ro2[64:96, :], mask)
                        return eng.tensor_tensor(qh[hp][64:96, cs],
                                                 ro1[64:96, :],
                                                 ro3[64:96, :], ALU.add)
                    P.do("vector", f, waits=[("tD", t_qr), ("qS", q_tabs)],
                         inc=[("vD", 1)], tag=f"qrope{g}_{tt}_{hp}")
                    v_pevac[("qr", bank)] = P.at("vD")
                if sub < 1.85:
                    t_projuse[tt] = P.at("tD")
                    continue
                for hp in range(HG):
                    bank = 4 if hp % 2 == 0 else 6
                    for lc in range(2):
                        w = list(basew) if lc == 0 else []
                        if lc == 0:
                            w.append(("sD", s_pevac.get(("kc", bank), 0)))

                        def f(eng, hp=hp, lc=lc, tt=tt, bank=bank):
                            return eng.matmul(
                                pb[bank][0:64, 0:TTW],
                                wku_g[:, lc * 256 + 64 * hp:
                                      lc * 256 + 64 * hp + 64],
                                latent[:, lc * T + tt * TTW:
                                       lc * T + (tt + 1) * TTW],
                                start=(lc == 0), stop=(lc == 1))
                        P.do("tensor", f, waits=w, inc=[("tD", 1)],
                             tag=f"kc{g}_{tt}_{hp}_{lc}")
                    t_kc = P.at("tD")

                    def f(eng, hp=hp, bank=bank, cs=cs):
                        return eng.activation(
                            kh[hp][0:64, cs], pb[bank][0:64, 0:TTW],
                            AF.Copy, scale=ISCALE)
                    P.do("scalar", f, waits=[("tD", t_kc)],
                         inc=[("sD", 1)], tag=f"kcev{g}_{tt}_{hp}")
                    s_pevac[("kc", bank)] = P.at("sD")
                if sub < 1.95:
                    t_projuse[tt] = P.at("tD")
                    continue
                for ts_ in range(4):
                    chunk = 4 * tt + ts_
                    bank = 5 if ts_ % 2 == 0 else 7
                    for lc in range(2):
                        w = list(basew) if lc == 0 else []
                        if lc == 0:
                            w += [("vD", v_vones),
                                  ("vD", v_pevac.get(("v", bank), 0))]

                        def f(eng, ts_=ts_, lc=lc, tt=tt, bank=bank):
                            return eng.matmul(
                                pb[bank][:, 0:256],
                                latent[:, lc * T + tt * TTW + ts_ * 128:
                                       lc * T + tt * TTW + (ts_ + 1) * 128],
                                wvu_g[:, lc * 256:(lc + 1) * 256],
                                start=(lc == 0), stop=(lc == 1))
                        P.do("tensor", f, waits=w, inc=[("tD", 1)],
                             tag=f"v{g}_{tt}_{ts_}_{lc}")
                    t_v = P.at("tD")

                    def f(eng, chunk=chunk, bank=bank):
                        return eng.tensor_copy(
                            v_g4[:, chunk:chunk + 1, :, 0:64],
                            pb[bank][:, 0:256]
                            .rearrange("p (o h n) -> p o h n", o=1, h=4))
                    P.do("vector", f, waits=[("tD", t_v)], inc=[("vD", 1)],
                         tag=f"vev{g}_{tt}_{ts_}")
                    v_pevac[("v", bank)] = P.at("vD")
                t_projuse[tt] = P.at("tD")
            t_projmm_done = P.at("tD")
            s_proj_done = P.at("sD")
            v_proj_done = P.at("vD")

            for j in range(HG):
                def f(eng, j=j):
                    return eng.tensor_copy(kh[j][64:96, :], krope[64:96, :])
                P.do("vector", f,
                     waits=[("vD", v_krope_all), ("tD", t_flash_done)],
                     inc=[("vD", 1)], tag=f"krcp{g}_{j}")
            v_krcp = P.at("vD")

            if sub < 2.5:
                continue
            sctr = 0
            v_after_s = {}
            qctr = 0
            s_after_quad = {}
            t_after_pv = {}
            y_stats = {}
            y_scr_free = {0: 0, 1: 0}
            for qt in range(QTN):
                vk = (qt + 1) * 128
                yslot = qt % 2
                for hp in range(HG):
                    qs = slice(qt * 128, (qt + 1) * 128)
                    nks = (vk + 511) // 512
                    for ks in range(nks):
                        par = sctr % 2
                        n0 = ks * 512
                        n1 = min(vk, n0 + 512)
                        w = [("sD", s_proj_done), ("vD", v_krcp),
                             ("vD", v_after_s.get(sctr - 2, 0))]

                        def f(eng, hp=hp, par=par, n0=n0, n1=n1, qs=qs):
                            return eng.matmul(
                                pb[par][:, 0:n1 - n0], qh[hp][:, qs],
                                kh[hp][:, n0:n1], start=True, stop=True)
                        P.do("tensor", f, waits=w, inc=[("tD", 1)],
                             tag=f"S{g}_{qt}_{hp}_{ks}")
                        t_s = P.at("tD")
                        mcol = 4 * hp + ks
                        last = (n1 == vk)

                        def f(eng, par=par, wd=n1 - n0, mcol=mcol,
                              last=last):
                            if last:
                                eng.tensor_tensor(
                                    pb[par][:, wd - 128:wd],
                                    pb[par][:, wd - 128:wd],
                                    mask0, ALU.add)
                                if wd < 384:
                                    eng.nop(cycle_cnt=300)
                            return eng.tensor_reduce(
                                mpart[:, mcol:mcol + 1],
                                pb[par][:, 0:wd], axis=AX.X, op=ALU.max)
                        P.do("vector", f, waits=[("tD", t_s)],
                             inc=[("vD", 1)], tag=f"Smax{g}_{qt}_{hp}_{ks}")
                        v_after_s[sctr] = P.at("vD")
                        sctr += 1
                    nq = (qt + 4) // 4
                    for iq in range(nq):
                        tp = qctr % 2
                        kcs = list(range(4 * iq, min(qt + 1, 4 * iq + 4)))
                        w = [("sD", s_proj_done), ("vD", v_krcp),
                             ("sD", s_after_quad.get(qctr - 2, 0))]

                        def f(eng, hp=hp, tp=tp, kcs=kcs, qs=qs):
                            r = None
                            for idx, kc in enumerate(kcs):
                                r = eng.matmul(
                                    pb[2 + tp][:, idx * 128:(idx + 1) * 128],
                                    kh[hp][:, kc * 128:(kc + 1) * 128],
                                    qh[hp][:, qs], start=True, stop=True)
                            return r
                        P.do("tensor", f, waits=w, inc=[("tD", 1)],
                             tag=f"ST{g}_{qt}_{hp}_{iq}")
                        t_st = P.at("tD")
                        w = [("tD", t_st),
                             ("tD", t_after_pv.get(qctr - 2, 0))]

                        def f(eng, tp=tp, n=len(kcs) * 128):
                            return eng.activation(ptb[tp][:, 0:n],
                                                  pb[2 + tp][:, 0:n], AF.Exp)
                        P.do("scalar", f, waits=w, inc=[("sD", 1)],
                             tag=f"exp{g}_{qt}_{hp}_{iq}")
                        s_after_quad[qctr] = P.at("sD")
                        s_exp = P.at("sD")
                        v_dmask = 0
                        if kcs[-1] == qt:
                            idx = len(kcs) - 1

                            def f(eng, tp=tp, idx=idx):
                                return eng.tensor_tensor(
                                    ptb[tp][:, idx * 128:(idx + 1) * 128],
                                    ptb[tp][:, idx * 128:(idx + 1) * 128],
                                    maskt01, ALU.mult)
                            P.do("vector", f, waits=[("sD", s_exp)],
                                 inc=[("vD", 1)], tag=f"dmask{g}_{qt}_{hp}")
                            v_dmask = P.at("vD")
                        w = [("sD", s_exp)]
                        if v_dmask:
                            w.append(("vD", v_dmask))
                        if iq == 0 and hp == 0:
                            w.append(("vD", y_stats.get(yslot, 0)))

                        def f(eng, hp=hp, tp=tp, kcs=kcs, iq=iq,
                              yslot=yslot, qt=qt):
                            r = None
                            for idx, kc in enumerate(kcs):
                                base = kc * (HG * 65) + 65 * hp
                                r = eng.matmul(
                                    pb[6 + yslot][:, 65 * hp:65 * hp + 65],
                                    ptb[tp][:, idx * 128:(idx + 1) * 128],
                                    v_g[:, base:base + 65],
                                    start=(iq == 0 and idx == 0),
                                    stop=(kc == qt))
                            return r
                        P.do("tensor", f, waits=w, inc=[("tD", 1)],
                             tag=f"PV{g}_{qt}_{hp}_{iq}")
                        t_after_pv[qctr] = P.at("tD")
                        qctr += 1
                t_pv_all = P.at("tD")
                nks = (vk + 511) // 512
                v_lastmax = P.at("vD")

                def f(eng, yslot=yslot):
                    return eng.tensor_copy(yscr[yslot],
                                           pb[6 + yslot][:, 0:260])
                P.do("vector", f,
                     waits=[("tD", t_pv_all), ("sD", y_scr_free[yslot])],
                     inc=[("vD", 1)], tag=f"ycp{g}_{qt}")
                y_stats[yslot] = P.at("vD")

                def f(eng, nks=nks, yslot=yslot, g=g):
                    eng.tensor_reduce(
                        m4v[yslot].rearrange("p (a b) -> p a b", b=1),
                        mpart.rearrange("p (h k) -> p h k", h=4)[:, :, 0:nks],
                        axis=AX.X, op=ALU.max)
                    eng.nop(cycle_cnt=300)
                    return eng.tensor_tensor(dn1[yslot], m4v[yslot],
                                             cbc[:, 4 * g:4 * (g + 1)],
                                             ALU.add)
                P.do("vector", f, waits=[("vD", v_lastmax)], inc=[("vD", 1)],
                     tag=f"m4{g}_{qt}")
                v_m4 = P.at("vD")
                P.do("scalar",
                     lambda eng, yslot=yslot: eng.activation(
                         emv[yslot], dn1[yslot], AF.Exp),
                     waits=[("vD", v_m4)], inc=[("sD", 1)], tag=f"em{g}_{qt}")
                s_em = P.at("sD")

                def f(eng, yslot=yslot):
                    eng.tensor_copy(zcv[yslot],
                                    yscr[yslot][:, 64:64 + 3 * 65 + 1:65])
                    return None
                P.do("vector", f, waits=[("vD", v_m4)], inc=[("vD", 1)],
                     tag=f"zc{g}_{qt}")

                def f(eng, yslot=yslot):
                    eng.tensor_tensor(dn2[yslot], emv[yslot], zcv[yslot],
                                      ALU.add)
                    eng.nop(cycle_cnt=300)
                    return eng.reciprocal(rdv[yslot], dn2[yslot])
                P.do("vector", f, waits=[("sD", s_em)],
                     inc=[("vD", 1)], tag=f"den{g}_{qt}")
                v_rden = P.at("vD")
                for hp in range(HG):
                    h = g * HG + hp
                    dst = y_sb[:, qt * E + 64 * h:qt * E + 64 * (h + 1)]

                    def f(eng, hp=hp, dst=dst, yslot=yslot):
                        return eng.activation(
                            dst, yscr[yslot][:, 65 * hp:65 * hp + 64],
                            AF.Copy, scale=rdv[yslot][:, hp:hp + 1])
                    P.do("scalar", f, waits=[("vD", v_rden)],
                         inc=[("sD", 1)], tag=f"yn{g}_{qt}_{hp}")
                y_scr_free[yslot] = P.at("sD")
            if debug and g == 0:
                P.do("sync", fdma(dbg["yscr"][:, 0:260], yscr[0]),
                     waits=[("vD", P.at("vD")), ("sD", P.at("sD"))],
                     inc=[("qS", 16)], tag="dbgyscr0")
                P.do("sync", fdma(dbg["yscr"][:, 260:520], yscr[1]),
                     waits=[("vD", P.at("vD")), ("sD", P.at("sD"))],
                     inc=[("qS", 16)], tag="dbgyscr1")
                def fst(eng):
                    eng.dma_start(out=dbg["st4"][:, 0:4], in_=m4v[1])
                    eng.dma_start(out=dbg["st4"][:, 4:8], in_=emv[1])
                    eng.dma_start(out=dbg["st4"][:, 8:12], in_=dn2[1])
                    eng.dma_start(out=dbg["st4"][:, 12:16], in_=rdv[1])
                    return eng.dma_start(out=dbg["st4"][:, 16:20], in_=zcv[1])
                P.do("sync", fst, waits=[("vD", P.at("vD")),
                                         ("sD", P.at("sD"))],
                     inc=[("qS", 80)], tag="dbgst4")
                P.do("sync", fdma(dbg["qh0"], qh[0][0:96, :]),
                     waits=[("sD", P.at("sD")), ("vD", P.at("vD"))],
                     inc=[("qS", 16)], tag="dbgqh0")
                P.do("sync", fdma(dbg["kh0"], kh[0][0:96, :]),
                     waits=[("sD", P.at("sD")), ("vD", P.at("vD"))],
                     inc=[("qS", 16)], tag="dbgkh0")
                P.do("sync", fdma(dbg["vg"], v_g),
                     waits=[("vD", P.at("vD"))],
                     inc=[("qS", 16)], tag="dbgvg")
            t_flash_done = P.at("tD")
            s_norm_done = P.at("sD")

        if debug:
            P.do("sync", fdma(dbg["y"], y_sb),
                 waits=[("sD", P.at("sD"))], inc=[("qS", 16)], tag="dbgy")
            P.do("sync", fdma(dbg["lat"], latent),
                 waits=[("sD", P.at("sD"))], inc=[("qS", 16)], tag="dbglat")
            P.do("sync", fdma(dbg["kro"], krope[0:96, :]),
                 waits=[("vD", P.at("vD"))], inc=[("qS", 16)], tag="dbgkro")
        P.barrier()
        A.freeit("xnb0", "xnb1", "wq_g", "wqr_g", "wku_g", "wvu_g",
                 "ctab", "stab2", "latent", "krope", "ro1", "ro2", "ro3",
                 "v_g", "ptb0", "ptb1", "yscr0", "yscr1", "mpart",
                 *[f"qh{j}" for j in range(HG)],
                 *[f"kh{j}" for j in range(HG)],
                 *[f"{nm}{j}" for nm in
                   ("m4v", "emv", "zcv", "dn1", "dn2", "rdv")
                   for j in range(2)])
        st["w1t2"] = [A.alloc(f"w1u{h}", 4 * HF * 2, BF16, [128, 4 * HF])
                      for h in range(2)]
        for h in range(2):
            P.do("sync",
                 fdma(st["w1t2"][h], w1bfd[:, h * 4 * HF:(h + 1) * 4 * HF]),
                 waits=[st["w1cache_q"]], inc=[("qS", 16)],
                 tag=f"w1prefetch{h}")
        st["w1t2_ready"] = ("qS", P.at("qS"))
        if sub < 4.5:
            A.freeit("y", "v_g", "mpart", "st4", "ptb0", "ptb1",
                     "mask0", "maskt01", "cbc",
                     *[f"qh{j}" for j in range(HG)],
                     *[f"kh{j}" for j in range(HG)])
            P.barrier()
            return
        wo_sb = A.alloc("wo_sb", CE * E * 2, BF16, [128, CE * E])
        yt_sb = A.alloc("yt_sb", CE * TTW * 2, BF16, [128, CE * TTW])
        P.do("gpsimd", fdma(wo_sb.rearrange("p (c n) -> p c n", c=CE),
                            wod.rearrange("(c p) n -> p c n", p=128)),
             inc=[("qG", 16)], tag="ld_wo")
        q_wo = P.at("qG")

        s_ytev = {}
        tctr = 0
        for tt in range(NTT):
            for fc in range(CE):
                for qs_ in range(4):
                    qtg = 4 * tt + qs_
                    bank = 2 if tctr % 2 == 0 else 4
                    slot = (tctr // 2) % 2
                    w = [("sD", s_ytev.get(tctr - 2, 0))]
                    if tctr == 0:
                        w += [("sD", s_norm_done)]
                    if fc == 0 and qs_ == 0:
                        w += [("vD", st["prev_resid"])]

                    def f(eng, fc=fc, qtg=qtg, slot=slot, bank=bank):
                        return eng.matmul(
                            pb[bank][:, slot * 128:(slot + 1) * 128],
                            y_sb[:, qtg * E + fc * 128:
                                 qtg * E + (fc + 1) * 128],
                            ident, start=True, stop=True)
                    P.do("tensor", f, waits=w, inc=[("tD", 1)],
                         tag=f"ytT{tt}_{fc}_{qs_}")
                    t_yt = P.at("tD")

                    def f(eng, fc=fc, qs_=qs_, slot=slot, bank=bank):
                        return eng.copy(
                            yt_sb[:, fc * TTW + qs_ * 128:
                                  fc * TTW + (qs_ + 1) * 128],
                            pb[bank][:, slot * 128:(slot + 1) * 128])
                    P.do("scalar", f, waits=[("tD", t_yt)], inc=[("sD", 1)],
                         tag=f"ytev{tt}_{fc}_{qs_}")
                    s_ytev[tctr] = P.at("sD")
                    tctr += 1
            s_yt_all = P.at("sD")
            for kc in range(CE):
                w = [("sD", s_yt_all), ("qG", q_wo)]
                if kc == 0:
                    w.append(("vD", st["prev_resid"]))

                def f(eng, kc=kc):
                    r = None
                    for m in range(8):
                        r = eng.matmul(
                            pb[m],
                            wo_sb[:, kc * E + m * 128:kc * E + (m + 1) * 128],
                            yt_sb[:, kc * TTW:(kc + 1) * TTW],
                            start=(kc == 0), stop=(kc == CE - 1))
                    return r
                P.do("tensor", f, waits=w, inc=[("tD", 1)],
                     tag=f"wo{tt}_{kc}")
            t_wo = P.at("tD")
            for m in range(8):
                dst = stream[:, m * T + tt * TTW:m * T + (tt + 1) * TTW]

                def f(eng, m=m, dst=dst):
                    return eng.tensor_tensor(dst, dst, pb[m], ALU.add)
                P.do("vector", f, waits=[("tD", t_wo)], inc=[("vD", 1)],
                     tag=f"wores{tt}_{m}")
            st["prev_resid"] = P.at("vD")

        A.freeit("y", "wo_sb", "yt_sb", "mask0", "maskt01", "cbc")
        P.barrier()

    emit_ffn(0)
    if stage >= 2:
        emit_attn(sub=sub)
    if stage >= 3:
        P.barrier()
        emit_ffn(1)

    def f(eng):
        return eng.dma_start(out=outT.rearrange("(c p) n -> p c n", p=128),
                             in_=stream.rearrange("p (c n) -> p c n", c=CE))
    P.do("sync", f, waits=[("vD", st["prev_resid"])], inc=[("qS", 16)],
         tag="st_out")
    P.do("sync", None, waits=[("qS", P.at("qS"))], tag="out_wait")

    P.check_deadlock()

    semctx = {n: nc.semaphore(n) for n in SEMS}
    semh = {n: c.__enter__() for n, c in semctx.items()}
    with nc.Block() as block:
        def runner(engname):
            def run(eng):
                seen = {}
                for waits, inc, fn, tag in P.ops[engname]:
                    for s, v in waits:
                        if seen.get(s, -1) >= v:
                            continue
                        seen[s] = v
                        eng.wait_ge(semh[s], v)
                    instr = fn(eng) if fn is not None else None
                    if inc:
                        if instr is None:
                            for s, a in inc:
                                eng.sem_inc(semh[s], a)
                        else:
                            for s, a in inc:
                                instr = instr.then_inc(semh[s], a)
            return run

        block.tensor(runner("tensor"))
        block.vector(runner("vector"))
        block.scalar(runner("scalar"))
        block.gpsimd(runner("gpsimd"))
        block.sync(runner("sync"))
    return nc


def _host_consts():
    ident = np.eye(128, dtype=ml_dtypes.bfloat16)
    i_ = np.arange(128)
    mask0 = np.where(i_[None, :] <= i_[:, None], 0.0, NEG).astype(np.float32)
    maskt01 = (i_[:, None] <= i_[None, :]).astype(ml_dtypes.bfloat16)
    d = np.arange(16, dtype=np.float64)
    inv = 1.0 / (10000.0 ** (d / 16.0))
    t = np.arange(T, dtype=np.float64)
    ang = t[None, :] * inv[:, None]
    c16, s16 = np.cos(ang), np.sin(ang)
    c32 = np.concatenate([c16, c16], axis=0)
    s2 = np.concatenate([s16, -s16], axis=0)
    ctab = np.tile(c32, (4, 1)).astype(ml_dtypes.bfloat16)
    stab2 = np.tile(s2, (4, 1)).astype(ml_dtypes.bfloat16)
    return ident, mask0, maskt01, ctab, stab2


def kernel(L0_attn_x, L0_attn_Wq, L0_attn_Wqr, L0_attn_Wkr, L0_attn_Wkvd,
           L0_attn_Wku, L0_attn_Wvu, L0_attn_Wo, L0_attn_lobo_log,
           L0_ffn_W1, L0_ffn_b1, L0_ffn_W2, L0_ffn_b2, L0_ffn_scale):
    x = np.asarray(L0_attn_x)
    B = x.shape[0]
    ident, mask0, maskt01, ctab, stab2 = _host_consts()
    shared = {
        "wq": np.ascontiguousarray(L0_attn_Wq, dtype=np.float32),
        "wqr": np.ascontiguousarray(L0_attn_Wqr, dtype=np.float32),
        "wkr": np.ascontiguousarray(L0_attn_Wkr, dtype=np.float32),
        "wkvd": np.ascontiguousarray(L0_attn_Wkvd, dtype=np.float32),
        "wku": np.ascontiguousarray(L0_attn_Wku, dtype=np.float32),
        "wvu": np.ascontiguousarray(L0_attn_Wvu, dtype=np.float32),
        "wo": np.ascontiguousarray(L0_attn_Wo, dtype=np.float32),
        "w1t": np.ascontiguousarray(np.asarray(L0_ffn_W1).T, dtype=np.float32),
        "w2t": np.ascontiguousarray(np.asarray(L0_ffn_W2).T, dtype=np.float32),
        "b1l": np.ascontiguousarray(
            np.asarray(L0_ffn_b1, dtype=np.float32).reshape(HC, 128).T),
        "b2row": np.asarray(L0_ffn_b2, dtype=np.float32).reshape(1, E).copy(),
        "scalebc": np.broadcast_to(
            np.asarray(L0_ffn_scale, dtype=np.float32).reshape(1, 1),
            (128, 1)).copy(),
        "lobobc": np.broadcast_to(
            np.asarray(L0_attn_lobo_log, dtype=np.float32).reshape(1, NH),
            (128, NH)).copy(),
        "ident": ident, "mask0": mask0, "maskt01": maskt01,
        "ctab": ctab, "stab2": stab2,
    }
    nc = build_program()
    in_maps = []
    for b in range(B):
        m = dict(shared)
        m["xT"] = np.ascontiguousarray(x[b].T, dtype=np.float32)
        in_maps.append(m)
    res = run_bass_kernel_spmd(nc, in_maps, core_ids=list(range(B)))
    out = np.stack([np.asarray(r["outT"]).T for r in res.results], axis=0)
    return np.ascontiguousarray(out, dtype=np.float32)
